# version 01
# baseline (speedup 1.0000x reference)
import numpy as np
import ml_dtypes
from contextlib import ExitStack
import concourse.bass as bass
import concourse.mybir as mybir
from concourse.bass_utils import run_bass_kernel_spmd

F32 = mybir.dt.float32
BF16 = mybir.dt.bfloat16
AF = mybir.ActivationFunctionType
ALU = mybir.AluOpType
AX = mybir.AxisListType
BF = ml_dtypes.bfloat16

B, T, D = 4, 4096, 1024
ROWS = B * T
NCORE = 8
RPC = ROWS // NCORE
NH0, NG0, DH = 16, 4, 64
EPS = 1e-5

SWAPMASK = [i ^ 1 for i in range(32)]



class Ser:

    def __init__(self):
        self.steps = []

    def s(self, engine, fn, inc=1):
        self.steps.append((engine, fn, inc))

    def dma(self, fn):
        self.steps.append(("sync", fn, 16))

    def dmag(self, fn):
        self.steps.append(("gpsimd", fn, 16))

    def tensor(self, fn):
        self.s("tensor", fn)

    def vector(self, fn):
        self.s("vector", fn)

    def scalar(self, fn):
        self.s("scalar", fn)

    def gpsimd(self, fn):
        self.s("gpsimd", fn)

    def emit(self, nc):
        total = sum(i for _, _, i in self.steps)
        with nc.semaphore("ser") as sem, nc.Block() as block:
            def mk(engine_name):
                def f(eng):
                    acc = 0
                    for e, fn, inc in self.steps:
                        if e == engine_name:
                            eng.wait_ge(sem, acc)
                            fn(eng).then_inc(sem, inc)
                        acc += inc
                    if engine_name == "sync":
                        eng.wait_ge(sem, total)
                return f
            block.sync(mk("sync"))
            block.gpsimd(mk("gpsimd"))
            block.vector(mk("vector"))
            block.scalar(mk("scalar"))
            block.tensor(mk("tensor"))


def _run(nc, in_maps):
    res = run_bass_kernel_spmd(nc, in_maps, core_ids=list(range(NCORE)))
    return res.results



def _rope0_tables():
    p = np.arange(128)
    j = (p % 64) // 2
    freq = np.exp(-(2.0 * j / 64.0) * np.log(10000.0))
    t = np.tile(np.arange(T, dtype=np.float64), B)
    ang = freq[:, None] * t[None, :]
    cc = np.cos(ang)
    ss = np.sin(ang)
    ss[p % 2 == 0] *= -1.0
    return cc.astype(np.float32), ss.astype(np.float32)


def _rope1_table():
    import math
    dim_exp = 16
    freqs = 2.0 ** np.linspace(0.0, math.log2(5.0), dim_exp)
    coords = 1.0 + np.arange(T, dtype=np.float64)
    radians = coords[:, None] * freqs[None, :] * math.pi
    c = np.cos(radians)
    s = np.sin(radians)
    tbl = np.ones((128, T), np.float64)
    for pp in range(128):
        dd = pp % 64
        if dd < 32:
            sign = 1.0 if dd % 2 == 1 else -1.0
            tbl[pp] = s[:, dd % 16] + sign * c[:, dd % 16]
    return np.ascontiguousarray(np.tile(tbl, (1, B))).astype(np.float32)


def _diag_masks():
    d = np.arange(4)[:, None, None]
    pk = np.arange(128)[None, :, None]
    fq = np.arange(512)[None, None, :]
    return ((128 * d + pk) <= fq).astype(np.float32)


IDENT = np.eye(128, dtype=np.float32)
B2 = np.zeros((128, 2), np.float32)
B2[:64, 0] = 1.0
B2[64:, 1] = 1.0



def ln_tile(ser, sc, x_sb, out_sb, tiles, gain_col=None, bias_col=None):
    st = tiles["st"]
    mean, var, sd, rstd, m2 = (st[:, i:i + 1] for i in range(5))
    sq = tiles["sq"][:, :]
    x_sb = x_sb[:, :]
    out_sb = out_sb[:, :]
    ser.vector(lambda e, o=mean, i_=x_sb: e.reduce_sum(o, i_, AX.X))
    ser.vector(lambda e, o=mean: e.tensor_scalar_mul(o, o, 1.0 / D))
    ser.scalar(lambda e, o=sq, i_=x_sb, a=var: e.activation(o, i_, AF.Square, accum_out=a))
    ser.vector(lambda e, o=var: e.tensor_scalar_mul(o, o, 1.0 / D))
    ser.vector(lambda e, o=m2, i_=mean: e.tensor_tensor(o, i_, i_, ALU.mult))
    ser.vector(lambda e, o=var, i_=m2: e.tensor_tensor(o, o, i_, ALU.subtract))
    ser.vector(lambda e, o=var: e.tensor_scalar_add(o, o, EPS))
    ser.scalar(lambda e, o=sd, i_=var: e.activation(o, i_, AF.Sqrt))
    ser.vector(lambda e, o=rstd, i_=sd: e.reciprocal(o, i_))
    ser.vector(lambda e, o=out_sb, i_=x_sb, s1=mean, s2=rstd: e.tensor_scalar(
        out=o, in0=i_, scalar1=s1, scalar2=s2, op0=ALU.subtract, op1=ALU.mult))



def build_p1():
    nc = bass.Bass("TRN2", target_bir_lowering=False)
    xd = nc.dram_tensor("x_rows", [RPC, D], F32, kind="ExternalInput").ap()
    wq_d = nc.dram_tensor("Wq", [D, D], F32, kind="ExternalInput").ap()
    bq_d = nc.dram_tensor("bq", [1, D], F32, kind="ExternalInput").ap()
    wk_d = nc.dram_tensor("Wk", [D, 256], F32, kind="ExternalInput").ap()
    bk_d = nc.dram_tensor("bk", [1, 256], F32, kind="ExternalInput").ap()
    wv_d = nc.dram_tensor("Wv", [D, 256], F32, kind="ExternalInput").ap()
    bv_d = nc.dram_tensor("bv", [1, 256], F32, kind="ExternalInput").ap()
    cc_d = nc.dram_tensor("cc0", [128, RPC], F32, kind="ExternalInput").ap()
    ss_d = nc.dram_tensor("ss0", [128, RPC], F32, kind="ExternalInput").ap()
    id_d = nc.dram_tensor("ident", [128, 128], F32, kind="ExternalInput").ap()

    xn_o = nc.dram_tensor("xn", [RPC, D], BF16, kind="ExternalOutput").ap()
    qf_o = nc.dram_tensor("q_f", [D, RPC], BF16, kind="ExternalOutput").ap()
    kf_o = nc.dram_tensor("k_f", [256, RPC], BF16, kind="ExternalOutput").ap()
    vt_o = nc.dram_tensor("v_tok", [RPC, 256], BF16, kind="ExternalOutput").ap()

    ser = Ser()
    with ExitStack() as ctx:
        sb = lambda name, shape, dt=BF16: ctx.enter_context(nc.sbuf_tensor(name, shape, dt))
        ps = lambda name, shape, dt=F32: ctx.enter_context(nc.psum_tensor(name, shape, dt))

        wq = sb("wq_s", [128, 8, D])
        wk = sb("wk_s", [128, 8, 256])
        wv = sb("wv_s", [128, 8, 256])
        bq = sb("bq_s", [1, D])
        bk = sb("bk_s", [1, 256])
        bv = sb("bv_s", [1, 256])
        ones = sb("ones", [1, 512])
        cc = sb("cc_s", [128, RPC])
        ss = sb("ss_s", [128, RPC])
        ident = sb("ident_s", [128, 128])
        x_sb = sb("x_sb", [128, D], F32)
        sq = sb("sq", [128, D], F32)
        st = sb("st", [128, 8], F32)
        xn_sb = sb("xn_sb", [128, D])
        xnf = sb("xnf", [128, 8, RPC])
        qraw = sb("qraw", [128, 512])
        qshuf = sb("qshuf", [128, 512])
        t1 = sb("t1", [128, 512])
        t2 = sb("t2", [128, 512])
        stage = sb("stage", [128, RPC])
        v_sb = sb("v_sb", [128, 256])
        psA = ps("psA", [128, 512])
        psT = ps("psT", [128, 128], BF16)
        psV = ps("psV", [128, 256])

        ser.dmag(lambda e: e.dma_start(out=wq[:], in_=wq_d.rearrange("(c p) n -> p c n", p=128)))
        ser.dmag(lambda e: e.dma_start(out=wk[:], in_=wk_d.rearrange("(c p) n -> p c n", p=128)))
        ser.dmag(lambda e: e.dma_start(out=wv[:], in_=wv_d.rearrange("(c p) n -> p c n", p=128)))
        ser.dmag(lambda e: e.dma_start(out=bq[:], in_=bq_d[:]))
        ser.dmag(lambda e: e.dma_start(out=bk[:], in_=bk_d[:]))
        ser.dmag(lambda e: e.dma_start(out=bv[:], in_=bv_d[:]))
        ser.dmag(lambda e: e.dma_start(out=cc[:], in_=cc_d[:]))
        ser.dmag(lambda e: e.dma_start(out=ss[:], in_=ss_d[:]))
        ser.dmag(lambda e: e.dma_start(out=ident[:], in_=id_d[:]))
        ser.vector(lambda e: e.memset(ones[:], 1.0))

        tiles = {"sq": sq, "st": st}
        NT = RPC // 128
        for i in range(NT):
            r = slice(128 * i, 128 * i + 128)
            ser.dma(lambda e, r=r: e.dma_start(out=x_sb[:], in_=xd[r, :]))
            ln_tile(ser, None, x_sb, xn_sb, tiles)
            ser.dma(lambda e, r=r: e.dma_start(out=xn_o[r, :], in_=xn_sb[:]))
            for c in range(8):
                ser.tensor(lambda e, c=c: e.transpose(psT[:], xn_sb[:, 128 * c:128 * c + 128], ident[:]))
                ser.vector(lambda e, c=c, i=i: e.tensor_copy(xnf[:, c, 128 * i:128 * i + 128], psT[:]))

        NB = RPC // 512
        for (wsb, bsb, nm, out_d) in ((wq, bq, 8, qf_o), (wk, bk, 2, kf_o)):
            for m in range(nm):
                msl = slice(128 * m, 128 * m + 128)
                for n in range(NB):
                    nsl = slice(512 * n, 512 * n + 512)
                    for c in range(8):
                        ser.tensor(lambda e, c=c, msl=msl, nsl=nsl, wsb=wsb: e.matmul(
                            psA[:], wsb[:, c, msl], xnf[:, c, nsl],
                            start=(c == 0), stop=False))
                    ser.tensor(lambda e, msl=msl, bsb=bsb: e.matmul(
                        psA[:], bsb[:, msl], ones[:], start=False, stop=True))
                    ser.vector(lambda e: e.tensor_copy(qraw[:], psA[:]))
                    ser.vector(lambda e: e.stream_shuffle(qshuf[:], qraw[:], SWAPMASK))
                    ser.vector(lambda e, nsl=nsl: e.tensor_tensor(t1[:], qraw[:], cc[:, nsl], ALU.mult))
                    ser.vector(lambda e, nsl=nsl: e.tensor_tensor(t2[:], qshuf[:], ss[:, nsl], ALU.mult))
                    ser.vector(lambda e, nsl=nsl: e.tensor_tensor(stage[:, nsl], t1[:], t2[:], ALU.add))
                ser.dma(lambda e, msl=msl, out_d=out_d: e.dma_start(out=out_d[msl, :], in_=stage[:]))

        for i in range(NT):
            r = slice(128 * i, 128 * i + 128)
            for c in range(8):
                ser.tensor(lambda e, c=c, i=i: e.matmul(
                    psV[:], xnf[:, c, 128 * i:128 * i + 128], wv[:, c, :],
                    start=(c == 0), stop=False))
            ser.tensor(lambda e: e.matmul(psV[:], ones[:, 0:128], bv[:], start=False, stop=True))
            ser.vector(lambda e: e.tensor_copy(v_sb[:], psV[:]))
            ser.dma(lambda e, r=r: e.dma_start(out=vt_o[r, :], in_=v_sb[:]))

        ser.emit(nc)
    return nc


def run_p1(x_flat, inp):
    cc0, ss0 = _rope0_tables()
    nc = build_p1()
    in_maps = []
    for c in range(NCORE):
        r = slice(RPC * c, RPC * c + RPC)
        in_maps.append({
            "x_rows": np.ascontiguousarray(x_flat[r]),
            "Wq": np.asarray(inp["L0_attn_Wq"], np.float32),
            "bq": np.asarray(inp["L0_attn_bq"], np.float32).reshape(1, D),
            "Wk": np.asarray(inp["L0_attn_Wk"], np.float32),
            "bk": np.asarray(inp["L0_attn_bk"], np.float32).reshape(1, 256),
            "Wv": np.asarray(inp["L0_attn_Wv"], np.float32),
            "bv": np.asarray(inp["L0_attn_bv"], np.float32).reshape(1, 256),
            "cc0": np.ascontiguousarray(cc0[:, r]),
            "ss0": np.ascontiguousarray(ss0[:, r]),
            "ident": IDENT,
        })
    res = _run(nc, in_maps)
    xn = np.concatenate([np.asarray(r_["xn"]) for r_ in res], axis=0)
    q_f = np.concatenate([np.asarray(r_["q_f"]) for r_ in res], axis=1)
    k_f = np.concatenate([np.asarray(r_["k_f"]) for r_ in res], axis=1)
    v_tok = np.concatenate([np.asarray(r_["v_tok"]) for r_ in res], axis=0)
    return xn, q_f, k_f, v_tok



def kernel(**inputs):
    x = np.asarray(inputs["L0_attn_x"], np.float32).reshape(ROWS, D)
    xn, q_f, k_f, v_tok = run_p1(x, inputs)
    att0 = run_p2(q_f, k_f, v_tok)
    c_f = run_p3(xn, inputs)
    x1, xn1 = run_p4(x, xn, att0, c_f, inputs)
    nc_pre = build_subpre()
    nc_att = build_subatt()
    nc_post = build_subpost()
    q0, k0, v0 = run_subpre(nc_pre, xn1, inputs, 0)
    att_s0 = run_subatt(nc_att, q0, k0, v0)
    z2 = run_subpost(nc_post, xn1, att_s0, inputs, 0)
    q1, k1, v1 = run_subpre(nc_pre, z2, inputs, 1)
    att_s1 = run_subatt(nc_att, q1, k1, v1)
    z4 = run_subpost(nc_post, z2, att_s1, inputs, 1)
    out = run_p9(xn1, x1, z4, inputs)
    return out.reshape(B, T, D)



def build_p2():
    nc = bass.Bass("TRN2", target_bir_lowering=False)
    q_d = nc.dram_tensor("q_sl", [512, T], BF16, kind="ExternalInput").ap()
    k_d = nc.dram_tensor("k_sl", [128, T], BF16, kind="ExternalInput").ap()
    v_d = nc.dram_tensor("v_sl", [T, 128], BF16, kind="ExternalInput").ap()
    m_d = nc.dram_tensor("masks", [128, 4, 512], F32, kind="ExternalInput").ap()
    att_o = nc.dram_tensor("att_sl", [512, T], BF16, kind="ExternalOutput").ap()

    NKT = T // 128
    NQT = T // 512
    ser = Ser()
    with ExitStack() as ctx:
        sb = lambda name, shape, dt=BF16: ctx.enter_context(nc.sbuf_tensor(name, shape, dt))
        ps = lambda name, shape, dt=F32: ctx.enter_context(nc.psum_tensor(name, shape, dt))

        q_sb = sb("q_sb", [128, 4, T])
        k2 = sb("k2", [128, 2, T])
        v_sb = sb("v_sb", [128, NKT, 128])
        vaug = sb("vaug", [128, NKT, 130])
        mask = sb("mask", [128, 4, 512])
        sq = sb("sq_p2", [128, T])
        pa = sb("pa", [128, 512])
        pb = sb("pb", [128, 512])
        araw = sb("araw", [64, 512], F32)
        asb = sb("asb", [64, 512])
        rec = sb("rec", [1, 512], F32)
        onesb = sb("onesb", [128, 1])
        onesf = sb("onesf", [1, 128], F32)
        hsc = sb("hsc", [1, 16], F32)
        ksc = sb("ksc", [1, 4], F32)
        bias_sb = sb("bias_sb", [128, 8], F32)
        psS_A = ps("psS_A", [128, 512])
        psS_B = ps("psS_B", [128, 512])
        psO_A = ps("psO_A", [128, 512])
        psO_B = ps("psO_B", [128, 512])
        psR = ps("psR", [128, 512])
        psQ = ps("psQ", [128, 512])

        ser.dma(lambda e: e.dma_start(out=q_sb[:], in_=q_d.rearrange("(c p) t -> p c t", p=128)))
        for g in range(2):
            for half in range(2):
                ser.dma(lambda e, g=g, half=half: e.dma_start(
                    out=k2[64 * half:64 * half + 64, g, :], in_=k_d[64 * g:64 * g + 64, :]))
        ser.dma(lambda e: e.dma_start(out=v_sb[:], in_=v_d.rearrange("(kt p) d -> p kt d", p=128)))
        ser.dmag(lambda e: e.dma_start(out=mask[:], in_=m_d[:]))
        ser.vector(lambda e: e.memset(vaug[:], 1.0))
        for kt in range(NKT):
            ser.vector(lambda e, kt=kt: e.tensor_copy(vaug[:, kt, 0:64], v_sb[:, kt, 0:64]))
            ser.vector(lambda e, kt=kt: e.tensor_copy(vaug[:, kt, 65:129], v_sb[:, kt, 64:128]))
        ser.vector(lambda e: e.memset(onesb[:], 1.0))
        ser.vector(lambda e: e.memset(onesf[:], 1.0))
        ser.vector(lambda e: e.memset(hsc[:], 0.0))
        ser.vector(lambda e: e.memset(ksc[:], 0.0))

        for c in range(4):
            ser.scalar(lambda e, c=c: e.activation(sq[:], q_sb[:, c, :], AF.Square))
            for half in range(2):
                h = 2 * c + half
                pr = slice(64 * half, 64 * half + 64)
                for n in range(NQT):
                    nsl = slice(512 * n, 512 * n + 512)
                    ser.tensor(lambda e, pr=pr, nsl=nsl: e.matmul(
                        psQ[0:1, :], onesb[pr, :], sq[pr, nsl], start=True, stop=True,
                        tile_position=(pr.start, 0)))
                    ser.vector(lambda e: e.reduce_max(rec[0:1, 0:1], psQ[0:1, :], AX.X))
                    ser.vector(lambda e, h=h: e.tensor_tensor(
                        hsc[0:1, h:h + 1], hsc[0:1, h:h + 1], rec[0:1, 0:1], ALU.max))
        ser.scalar(lambda e: e.activation(sq[:], k2[:, 0, :], AF.Square))
        for g in range(2):
            pr = slice(64 * g, 64 * g + 64)
            for n in range(NQT):
                nsl = slice(512 * n, 512 * n + 512)
                ser.tensor(lambda e, pr=pr, nsl=nsl: e.matmul(
                    psQ[0:1, :], onesb[pr, :], sq[pr, nsl], start=True, stop=True,
                    tile_position=(pr.start, 0)))
                ser.vector(lambda e: e.reduce_max(rec[0:1, 0:1], psQ[0:1, :], AX.X))
                ser.vector(lambda e, g=g: e.tensor_tensor(
                    ksc[0:1, g:g + 1], ksc[0:1, g:g + 1], rec[0:1, 0:1], ALU.max))
        ser.scalar(lambda e: e.activation(hsc[0:1, 8:16], hsc[0:1, 0:8], AF.Sqrt))
        ser.scalar(lambda e: e.activation(ksc[0:1, 2:4], ksc[0:1, 0:2], AF.Sqrt))
        for h in range(8):
            g = h // 4
            ser.vector(lambda e, h=h, g=g: e.tensor_tensor(
                hsc[0:1, h:h + 1], hsc[0:1, 8 + h:9 + h], ksc[0:1, 2 + g:3 + g], ALU.mult))
            ser.vector(lambda e, h=h: e.tensor_scalar_mul(hsc[0:1, h:h + 1], hsc[0:1, h:h + 1], -0.125))
            ser.tensor(lambda e, h=h: e.matmul(
                psR[:, 0:1], onesf[:, :], hsc[0:1, h:h + 1], start=True, stop=True))
            ser.vector(lambda e, h=h: e.tensor_copy(bias_sb[:, h:h + 1], psR[:, 0:1]))

        for p in range(4):
            g = p // 2
            for j in range(NQT):
                jsl = slice(512 * j, 512 * j + 512)
                nkt = 4 * j + 4
                for i in range(nkt):
                    isl = slice(128 * i, 128 * i + 128)
                    last = i == nkt - 1
                    ser.tensor(lambda e, g=g, isl=isl, p=p, jsl=jsl: e.matmul(
                        psS_A[:], k2[0:64, g, isl], q_sb[0:64, p, jsl],
                        start=True, stop=True, tile_position=(0, 0)))
                    ser.tensor(lambda e, g=g, isl=isl, p=p, jsl=jsl: e.matmul(
                        psS_B[:], k2[64:128, g, isl], q_sb[64:128, p, jsl],
                        start=True, stop=True, tile_position=(64, 0)))
                    ser.scalar(lambda e, p=p: e.activation(
                        pa[:], psS_A[:], AF.Exp, bias=bias_sb[:, 2 * p:2 * p + 1], scale=0.125))
                    ser.scalar(lambda e, p=p: e.activation(
                        pb[:], psS_B[:], AF.Exp, bias=bias_sb[:, 2 * p + 1:2 * p + 2], scale=0.125))
                    d = i - 4 * j
                    if d >= 0:
                        ser.vector(lambda e, d=d: e.tensor_tensor(pa[:], pa[:], mask[:, d, :], ALU.mult))
                        ser.vector(lambda e, d=d: e.tensor_tensor(pb[:], pb[:], mask[:, d, :], ALU.mult))
                    ser.tensor(lambda e, i=i, last=last, g=g: e.matmul(
                        psO_A[0:65, :], vaug[:, i, 65 * g:65 * g + 65], pa[:], start=(i == 0), stop=last))
                    ser.tensor(lambda e, i=i, last=last, g=g: e.matmul(
                        psO_B[0:65, :], vaug[:, i, 65 * g:65 * g + 65], pb[:], start=(i == 0), stop=last))
                for half, psO in ((0, psO_A), (1, psO_B)):
                    h = 2 * p + half
                    ser.vector(lambda e, psO=psO: e.reciprocal(rec[:], psO[64:65, :]))
                    ser.tensor(lambda e: e.matmul(psR[0:64, :], onesf[:, 0:64], rec[:], start=True, stop=True))
                    ser.vector(lambda e, psO=psO: e.tensor_copy(araw[:], psO[0:64, :]))
                    ser.vector(lambda e: e.tensor_tensor(asb[:], araw[:], psR[0:64, :], ALU.mult))
                    ser.dma(lambda e, h=h, jsl=jsl: e.dma_start(
                        out=att_o[64 * h:64 * h + 64, jsl], in_=asb[:]))
        ser.emit(nc)
    return nc


def run_p2(q_f, k_f, v_tok):
    masks = np.ascontiguousarray(_diag_masks().transpose(1, 0, 2))
    nc = build_p2()
    in_maps = []
    for c in range(NCORE):
        b, hg = c // 2, c % 2
        in_maps.append({
            "q_sl": np.ascontiguousarray(q_f[512 * hg:512 * hg + 512, T * b:T * b + T]),
            "k_sl": np.ascontiguousarray(k_f[128 * hg:128 * hg + 128, T * b:T * b + T]),
            "v_sl": np.ascontiguousarray(v_tok[T * b:T * b + T, 128 * hg:128 * hg + 128]),
            "masks": masks,
        })
    res = _run(nc, in_maps)
    att = np.zeros((D, ROWS), BF)
    for c in range(NCORE):
        b, hg = c // 2, c % 2
        att[512 * hg:512 * hg + 512, T * b:T * b + T] = np.asarray(res[c]["att_sl"])
    return att



def emit_load_wT(ser, nc, w_d, stage, wT, ident, psT, nchunk_out=8, nchunk_in=8):
    ser.dmag(lambda e: e.dma_start(out=stage[:, 0:nchunk_out, :], in_=w_d.rearrange("(c p) n -> p c n", p=128)))
    for co in range(nchunk_out):
        for ci in range(nchunk_in):
            ser.tensor(lambda e, co=co, ci=ci: e.transpose(
                psT[:], stage[:, co, 128 * ci:128 * ci + 128], ident[:]))
            ser.vector(lambda e, co=co, ci=ci: e.tensor_copy(
                wT[:, ci, 128 * co:128 * co + 128], psT[:]))


def build_p3():
    nc = bass.Bass("TRN2", target_bir_lowering=False)
    xn_d = nc.dram_tensor("xn_sl", [4 * 512, D], BF16, kind="ExternalInput").ap()
    w_ds = {}
    for nm in ("wf", "uf", "wi", "ui", "wc", "uc"):
        w_ds[nm] = nc.dram_tensor(nm, [D, D], F32, kind="ExternalInput").ap()
    b_ds = {}
    for nm in ("bf", "bi", "bc"):
        b_ds[nm] = nc.dram_tensor(nm, [1, D], F32, kind="ExternalInput").ap()
    inc_d = nc.dram_tensor("in_c", [128, 8], F32, kind="ExternalInput").ap()
    id_d = nc.dram_tensor("ident", [128, 128], F32, kind="ExternalInput").ap()
    c_o = nc.dram_tensor("c_final", [D, 512], F32, kind="ExternalOutput").ap()

    ser = Ser()
    with ExitStack() as ctx:
        sb = lambda name, shape, dt=BF16: ctx.enter_context(nc.sbuf_tensor(name, shape, dt))
        ps = lambda name, shape, dt=F32: ctx.enter_context(nc.psum_tensor(name, shape, dt))

        ident = sb("ident_s", [128, 128])
        stage = sb("stage_w", [128, 8, D])
        wT = {nm: sb(nm + "_T", [128, 8, D]) for nm in w_ds}
        brow = {nm: sb(nm + "_r", [1, D]) for nm in b_ds}
        inc_sb = sb("inc_sb", [128, 8], F32)
        ones = sb("ones_p3", [1, 512])
        xf = sb("xf", [128, 8, 512])
        xld = sb("xld", [128, D])
        cf32 = sb("cf32", [128, 8, 512], F32)
        cb16 = sb("cb16", [128, 8, 512])
        cnew = sb("cnew", [128, 8, 512], F32)
        gsb = sb("gsb", [128, 8, 512], F32)
        gch = sb("gch", [128, 512], F32)
        tmp = sb("tmp_p3", [128, 512], F32)
        psA = ps("psA_p3", [128, 512])
        psT = ps("psT_p3", [128, 128], BF16)

        ser.dmag(lambda e: e.dma_start(out=ident[:], in_=id_d[:]))
        for nm in w_ds:
            emit_load_wT(ser, nc, w_ds[nm], stage, wT[nm], ident, psT)
        for nm in b_ds:
            ser.dmag(lambda e, nm=nm: e.dma_start(out=brow[nm][:], in_=b_ds[nm][:]))
        ser.dma(lambda e: e.dma_start(out=inc_sb[:], in_=inc_d[:]))
        ser.vector(lambda e: e.memset(ones[:], 1.0))

        for m in range(8):
            ser.vector(lambda e, m=m: e.memset(cf32[:, m, :], 0.0))
            ser.vector(lambda e, m=m: e.tensor_scalar_add(cf32[:, m, :], cf32[:, m, :], inc_sb[:, m:m + 1]))
            ser.vector(lambda e, m=m: e.tensor_copy(cb16[:, m, :], cf32[:, m, :]))

        def gate_psum(wname, uname, bname, m):
            for ci in range(8):
                ser.tensor(lambda e, ci=ci, m=m, wname=wname: e.matmul(
                    psA[:], wT[wname][:, ci, 128 * m:128 * m + 128], xf[:, ci, :],
                    start=(ci == 0), stop=False))
            for ci in range(8):
                ser.tensor(lambda e, ci=ci, m=m, uname=uname: e.matmul(
                    psA[:], wT[uname][:, ci, 128 * m:128 * m + 128], cb16[:, ci, :],
                    start=False, stop=False))
            ser.tensor(lambda e, m=m, bname=bname: e.matmul(
                psA[:], brow[bname][:, 128 * m:128 * m + 128], ones[:], start=False, stop=True))

        for t in range(4):
            for rt in range(4):
                r = slice(512 * t + 128 * rt, 512 * t + 128 * rt + 128)
                ser.dma(lambda e, r=r: e.dma_start(out=xld[:], in_=xn_d[r, :]))
                for c in range(8):
                    ser.tensor(lambda e, c=c: e.transpose(
                        psT[:], xld[:, 128 * c:128 * c + 128], ident[:]))
                    ser.vector(lambda e, c=c, rt=rt: e.tensor_copy(
                        xf[:, c, 128 * rt:128 * rt + 128], psT[:]))
            for m in range(8):
                gate_psum("wc", "uc", "bc", m)
                ser.scalar(lambda e, m=m: e.activation(gsb[:, m, :], psA[:], AF.Tanh))
            for m in range(8):
                gate_psum("wf", "uf", "bf", m)
                ser.scalar(lambda e, m=m: e.activation(gch[:], psA[:], AF.Sigmoid))
                ser.vector(lambda e, m=m: e.tensor_tensor(
                    cnew[:, m, :], gch[:], cf32[:, m, :], ALU.mult))
            for m in range(8):
                gate_psum("wi", "ui", "bi", m)
                ser.scalar(lambda e, m=m: e.activation(gch[:], psA[:], AF.Sigmoid))
                ser.vector(lambda e, m=m: e.tensor_tensor(tmp[:], gch[:], gsb[:, m, :], ALU.mult))
                ser.vector(lambda e, m=m: e.tensor_tensor(
                    cf32[:, m, :], cnew[:, m, :], tmp[:], ALU.add))
            for m in range(8):
                ser.vector(lambda e, m=m: e.tensor_copy(cb16[:, m, :], cf32[:, m, :]))

        for m in range(8):
            ser.dma(lambda e, m=m: e.dma_start(out=c_o[128 * m:128 * m + 128, :], in_=cf32[:, m, :]))
        ser.emit(nc)
    return nc


def run_p3(xn, inp):
    nc = build_p3()
    in_maps = []
    for c in range(NCORE):
        s0 = 512 * c
        xn_sl = np.concatenate([xn[t * T + s0:t * T + s0 + 512] for t in range(B)], axis=0)
        m = {"xn_sl": np.ascontiguousarray(xn_sl), "ident": IDENT,
             "in_c": np.ascontiguousarray(np.asarray(inp["L0_ffn_in_c"], np.float32).reshape(8, 128).T)}
        for nm in ("wf", "uf", "wi", "ui", "wc", "uc"):
            m[nm] = np.asarray(inp["L0_ffn_" + nm], np.float32)
        for nm in ("bf", "bi", "bc"):
            m[nm] = np.asarray(inp["L0_ffn_" + nm], np.float32).reshape(1, D)
        in_maps.append(m)
    res = _run(nc, in_maps)
    return np.concatenate([np.asarray(r_["c_final"]) for r_ in res], axis=1)



def build_p4():
    nc = bass.Bass("TRN2", target_bir_lowering=False)
    x_d = nc.dram_tensor("x_rows", [RPC, D], F32, kind="ExternalInput").ap()
    xn_d = nc.dram_tensor("xn_rows", [RPC, D], BF16, kind="ExternalInput").ap()
    att_d = nc.dram_tensor("att_sl", [D, RPC], BF16, kind="ExternalInput").ap()
    cf_d = nc.dram_tensor("cf_sl", [D, RPC], F32, kind="ExternalInput").ap()
    Wo_d = nc.dram_tensor("Wo", [D, D], F32, kind="ExternalInput").ap()
    bo_d = nc.dram_tensor("bo", [1, D], F32, kind="ExternalInput").ap()
    wof_d = nc.dram_tensor("wof", [D, D], F32, kind="ExternalInput").ap()
    uof_d = nc.dram_tensor("uof", [D, D], F32, kind="ExternalInput").ap()
    bof_d = nc.dram_tensor("bof", [1, D], F32, kind="ExternalInput").ap()
    id_d = nc.dram_tensor("ident", [128, 128], F32, kind="ExternalInput").ap()
    x1_o = nc.dram_tensor("x1", [RPC, D], F32, kind="ExternalOutput").ap()
    xn1_o = nc.dram_tensor("xn1", [RPC, D], F32, kind="ExternalOutput").ap()

    ser = Ser()
    with ExitStack() as ctx:
        sb = lambda name, shape, dt=BF16: ctx.enter_context(nc.sbuf_tensor(name, shape, dt))
        ps = lambda name, shape, dt=F32: ctx.enter_context(nc.psum_tensor(name, shape, dt))
        ident = sb("ident_s", [128, 128])
        stage = sb("stage_w", [128, 8, D])
        attf = sb("attf", [128, 8, RPC])
        cfb = sb("cfb", [128, 8, RPC])
        wo_sb = sb("wo_sb", [128, 8, D])
        woT = sb("woT", [128, 8, D])
        uoT = sb("uoT", [128, 8, D])
        bo_r = sb("bo_r", [1, D])
        bof_r = sb("bof_r", [1, D])
        ones = sb("ones_p4", [1, 512])
        x_sb = sb("x_sb", [128, D], F32)
        xn_sb = sb("xn_sb", [128, D])
        xnf = sb("xnf_t", [128, 8, 128])
        h_sb = sb("h_sb", [128, D], F32)
        x1_sb = sb("x1_sb", [128, D], F32)
        xn1_sb = sb("xn1_sb", [128, D], F32)
        sq = sb("sq", [128, D], F32)
        st = sb("st", [128, 8], F32)
        psO = ps("psO_p4", [128, 512])
        psH = ps("psH_p4", [128, 512])
        psT = ps("psT_p4", [128, 128], BF16)

        ser.dmag(lambda e: e.dma_start(out=ident[:], in_=id_d[:]))
        ser.dmag(lambda e: e.dma_start(out=attf[:], in_=att_d.rearrange("(c p) t -> p c t", p=128)))
        ser.dmag(lambda e: e.dma_start(out=cfb[:], in_=cf_d.rearrange("(c p) t -> p c t", p=128)))
        ser.dmag(lambda e: e.dma_start(out=wo_sb[:], in_=Wo_d.rearrange("(c p) n -> p c n", p=128)))
        ser.dmag(lambda e: e.dma_start(out=bo_r[:], in_=bo_d[:]))
        ser.dmag(lambda e: e.dma_start(out=bof_r[:], in_=bof_d[:]))
        emit_load_wT(ser, nc, wof_d, stage, woT, ident, psT)
        emit_load_wT(ser, nc, uof_d, stage, uoT, ident, psT)
        ser.vector(lambda e: e.memset(ones[:], 1.0))

        tiles = {"sq": sq, "st": st}
        NT = RPC // 128
        for i in range(NT):
            r = slice(128 * i, 128 * i + 128)
            tsl = slice(128 * i, 128 * i + 128)
            ser.dma(lambda e, r=r: e.dma_start(out=x_sb[:], in_=x_d[r, :]))
            ser.dma(lambda e, r=r: e.dma_start(out=xn_sb[:], in_=xn_d[r, :]))
            for c in range(8):
                ser.tensor(lambda e, c=c: e.transpose(psT[:], xn_sb[:, 128 * c:128 * c + 128], ident[:]))
                ser.vector(lambda e, c=c: e.tensor_copy(xnf[:, c, :], psT[:]))
            for nb in range(2):
                nsl = slice(512 * nb, 512 * nb + 512)
                for c in range(8):
                    ser.tensor(lambda e, c=c, tsl=tsl, nsl=nsl: e.matmul(
                        psO[:], attf[:, c, tsl], wo_sb[:, c, nsl], start=(c == 0), stop=False))
                ser.tensor(lambda e, nsl=nsl: e.matmul(
                    psO[:], ones[:, 0:128], bo_r[:, nsl], start=False, stop=True))
                for c in range(8):
                    ser.tensor(lambda e, c=c, nsl=nsl: e.matmul(
                        psH[:], xnf[:, c, :], woT[:, c, nsl], start=(c == 0), stop=False))
                for c in range(8):
                    ser.tensor(lambda e, c=c, tsl=tsl, nsl=nsl: e.matmul(
                        psH[:], cfb[:, c, tsl], uoT[:, c, nsl], start=False, stop=False))
                ser.tensor(lambda e, nsl=nsl: e.matmul(
                    psH[:], ones[:, 0:128], bof_r[:, nsl], start=False, stop=True))
                ser.scalar(lambda e, nsl=nsl: e.activation(h_sb[:, nsl], psH[:], AF.Tanh))
                ser.vector(lambda e, nsl=nsl: e.tensor_tensor(
                    x1_sb[:, nsl], x_sb[:, nsl], h_sb[:, nsl], ALU.add))
                ser.vector(lambda e, nsl=nsl: e.tensor_tensor(
                    x1_sb[:, nsl], x1_sb[:, nsl], psO[:], ALU.add))
            ser.dma(lambda e, r=r: e.dma_start(out=x1_o[r, :], in_=x1_sb[:]))
            ln_tile(ser, None, x1_sb, xn1_sb, tiles)
            ser.dma(lambda e, r=r: e.dma_start(out=xn1_o[r, :], in_=xn1_sb[:]))
        ser.emit(nc)
    return nc


def run_p4(x_flat, xn, att_f, c_f, inp):
    nc = build_p4()
    in_maps = []
    for c in range(NCORE):
        r = slice(RPC * c, RPC * c + RPC)
        b, half = c // 2, c % 2
        tcols = slice(2048 * half, 2048 * half + 2048)
        in_maps.append({
            "x_rows": np.ascontiguousarray(x_flat[r]),
            "xn_rows": np.ascontiguousarray(xn[r]),
            "att_sl": np.ascontiguousarray(att_f[:, r]),
            "cf_sl": np.ascontiguousarray(c_f[:, tcols]),
            "Wo": np.asarray(inp["L0_attn_Wo"], np.float32),
            "bo": np.asarray(inp["L0_attn_bo"], np.float32).reshape(1, D),
            "wof": np.asarray(inp["L0_ffn_wo"], np.float32),
            "uof": np.asarray(inp["L0_ffn_uo"], np.float32),
            "bof": np.asarray(inp["L0_ffn_bo"], np.float32).reshape(1, D),
            "ident": IDENT,
        })
    res = _run(nc, in_maps)
    x1 = np.concatenate([np.asarray(r_["x1"]) for r_ in res], axis=0)
    xn1 = np.concatenate([np.asarray(r_["xn1"]) for r_ in res], axis=0)
    return x1, xn1



def build_subpre():
    nc = bass.Bass("TRN2", target_bir_lowering=False)
    z_d = nc.dram_tensor("z_rows", [RPC, D], F32, kind="ExternalInput").ap()
    g_d = nc.dram_tensor("g_row", [1, D], F32, kind="ExternalInput").ap()
    b_d = nc.dram_tensor("b_row", [1, D], F32, kind="ExternalInput").ap()
    wq_d = nc.dram_tensor("wq", [D, D], F32, kind="ExternalInput").ap()
    wk_d = nc.dram_tensor("wk", [D, D], F32, kind="ExternalInput").ap()
    wv_d = nc.dram_tensor("wv", [D, D], F32, kind="ExternalInput").ap()
    bq_d = nc.dram_tensor("bq", [1, D], F32, kind="ExternalInput").ap()
    bk_d = nc.dram_tensor("bk", [1, D], F32, kind="ExternalInput").ap()
    bv_d = nc.dram_tensor("bv", [1, D], F32, kind="ExternalInput").ap()
    tb_d = nc.dram_tensor("tbl", [128, RPC], F32, kind="ExternalInput").ap()
    id_d = nc.dram_tensor("ident", [128, 128], F32, kind="ExternalInput").ap()
    qf_o = nc.dram_tensor("q_f", [D, RPC], BF16, kind="ExternalOutput").ap()
    kf_o = nc.dram_tensor("k_f", [D, RPC], BF16, kind="ExternalOutput").ap()
    vt_o = nc.dram_tensor("v_tok", [RPC, D], BF16, kind="ExternalOutput").ap()

    ser = Ser()
    with ExitStack() as ctx:
        sb = lambda name, shape, dt=BF16: ctx.enter_context(nc.sbuf_tensor(name, shape, dt))
        ps = lambda name, shape, dt=F32: ctx.enter_context(nc.psum_tensor(name, shape, dt))
        ident = sb("ident_s", [128, 128])
        wq = sb("wq_s", [128, 8, D])
        wk = sb("wk_s", [128, 8, D])
        wv = sb("wv_s", [128, 8, D])
        bq = sb("bq_s", [1, D])
        bk = sb("bk_s", [1, D])
        bv = sb("bv_s", [1, D])
        tbl = sb("tbl_s", [128, RPC])
        gb = sb("gb", [128, D], F32)
        bb = sb("bb", [128, D], F32)
        onesf = sb("onesf", [1, 128], F32)
        ones = sb("ones_sp", [1, 512])
        z_sb = sb("z_sb", [128, D], F32)
        zl1 = sb("zl1", [128, D], F32)
        zl_sb = sb("zl_sb", [128, D])
        zlf = sb("zlf", [128, 8, RPC])
        sq = sb("sq", [128, D], F32)
        st = sb("st", [128, 8], F32)
        qraw = sb("qraw", [128, 512])
        stage = sb("stage_q", [128, RPC])
        v_sb = sb("v_sb", [128, D])
        psA = ps("psA_sp", [128, 512])
        psT = ps("psT_sp", [128, 128], BF16)

        ser.dmag(lambda e: e.dma_start(out=ident[:], in_=id_d[:]))
        for w, wd in ((wq, wq_d), (wk, wk_d), (wv, wv_d)):
            ser.dmag(lambda e, w=w, wd=wd: e.dma_start(out=w[:], in_=wd.rearrange("(c p) n -> p c n", p=128)))
        for b_, bd in ((bq, bq_d), (bk, bk_d), (bv, bv_d)):
            ser.dmag(lambda e, b_=b_, bd=bd: e.dma_start(out=b_[:], in_=bd[:]))
        ser.dmag(lambda e: e.dma_start(out=tbl[:], in_=tb_d[:]))
        ser.vector(lambda e: e.memset(onesf[:], 1.0))
        ser.vector(lambda e: e.memset(ones[:], 1.0))
        gbounce = sb("gbounce", [1, D], F32)
        for row_d, dst in ((g_d, gb), (b_d, bb)):
            ser.dma(lambda e, row_d=row_d: e.dma_start(out=gbounce[:], in_=row_d[:]))
            for nb in range(2):
                nsl = slice(512 * nb, 512 * nb + 512)
                ser.tensor(lambda e, nsl=nsl: e.matmul(
                    psA[:], onesf[:, :], gbounce[:, nsl], start=True, stop=True))
                ser.vector(lambda e, dst=dst, nsl=nsl: e.tensor_copy(dst[:, nsl], psA[:]))

        tiles = {"sq": sq, "st": st}
        NT = RPC // 128
        for i in range(NT):
            r = slice(128 * i, 128 * i + 128)
            ser.dma(lambda e, r=r: e.dma_start(out=z_sb[:], in_=z_d[r, :]))
            ln_tile(ser, None, z_sb, zl1, tiles)
            ser.vector(lambda e: e.tensor_tensor(zl1[:], zl1[:], gb[:], ALU.mult))
            ser.vector(lambda e: e.tensor_tensor(zl_sb[:], zl1[:], bb[:], ALU.add))
            for c in range(8):
                ser.tensor(lambda e, c=c: e.transpose(psT[:], zl_sb[:, 128 * c:128 * c + 128], ident[:]))
                ser.vector(lambda e, c=c, i=i: e.tensor_copy(zlf[:, c, 128 * i:128 * i + 128], psT[:]))

        NB = RPC // 512
        for (wsb, bsb, out_d, rope) in ((wq, bq, qf_o, True), (wk, bk, kf_o, True)):
            for m in range(8):
                msl = slice(128 * m, 128 * m + 128)
                for n in range(NB):
                    nsl = slice(512 * n, 512 * n + 512)
                    for c in range(8):
                        ser.tensor(lambda e, c=c, msl=msl, nsl=nsl, wsb=wsb: e.matmul(
                            psA[:], wsb[:, c, msl], zlf[:, c, nsl], start=(c == 0), stop=False))
                    ser.tensor(lambda e, msl=msl, bsb=bsb: e.matmul(
                        psA[:], bsb[:, msl], ones[:], start=False, stop=True))
                    ser.vector(lambda e: e.tensor_copy(qraw[:], psA[:]))
                    ser.vector(lambda e, nsl=nsl: e.tensor_tensor(
                        stage[:, nsl], qraw[:], tbl[:, nsl], ALU.mult))
                ser.dma(lambda e, msl=msl, out_d=out_d: e.dma_start(out=out_d[msl, :], in_=stage[:]))
        for i in range(NT):
            r = slice(128 * i, 128 * i + 128)
            for nb in range(2):
                nsl = slice(512 * nb, 512 * nb + 512)
                for c in range(8):
                    ser.tensor(lambda e, c=c, i=i, nsl=nsl: e.matmul(
                        psA[:], zlf[:, c, 128 * i:128 * i + 128], wv[:, c, nsl],
                        start=(c == 0), stop=False))
                ser.tensor(lambda e, nsl=nsl: e.matmul(
                    psA[:], ones[:, 0:128], bv[:, nsl], start=False, stop=True))
                ser.vector(lambda e, nsl=nsl: e.tensor_copy(v_sb[:, nsl], psA[:]))
            ser.dma(lambda e, r=r: e.dma_start(out=vt_o[r, :], in_=v_sb[:]))
        ser.emit(nc)
    return nc


def run_subpre(nc, z, inp, li):
    tbl = _rope1_table()
    in_maps = []
    for c in range(NCORE):
        r = slice(RPC * c, RPC * c + RPC)
        in_maps.append({
            "z_rows": np.ascontiguousarray(z[r]),
            "g_row": np.asarray(inp["L1_attn_ln1_g"], np.float32)[li].reshape(1, D),
            "b_row": np.asarray(inp["L1_attn_ln1_b"], np.float32)[li].reshape(1, D),
            "wq": np.asarray(inp["L1_attn_wq"], np.float32)[li],
            "wk": np.asarray(inp["L1_attn_wk"], np.float32)[li],
            "wv": np.asarray(inp["L1_attn_wv"], np.float32)[li],
            "bq": np.asarray(inp["L1_attn_bq"], np.float32)[li].reshape(1, D),
            "bk": np.asarray(inp["L1_attn_bk"], np.float32)[li].reshape(1, D),
            "bv": np.asarray(inp["L1_attn_bv"], np.float32)[li].reshape(1, D),
            "tbl": np.ascontiguousarray(tbl[:, r]),
            "ident": IDENT,
        })
    res = _run(nc, in_maps)
    q_f = np.concatenate([np.asarray(r_["q_f"]) for r_ in res], axis=1)
    k_f = np.concatenate([np.asarray(r_["k_f"]) for r_ in res], axis=1)
    v_tok = np.concatenate([np.asarray(r_["v_tok"]) for r_ in res], axis=0)
    return q_f, k_f, v_tok



def build_subatt():
    nc = bass.Bass("TRN2", target_bir_lowering=False)
    q_d = nc.dram_tensor("q_sl", [512, T], BF16, kind="ExternalInput").ap()
    k_d = nc.dram_tensor("k_sl", [512, T], BF16, kind="ExternalInput").ap()
    v_d = nc.dram_tensor("v_sl", [T, 512], BF16, kind="ExternalInput").ap()
    att_o = nc.dram_tensor("att_sl", [512, T], BF16, kind="ExternalOutput").ap()

    NKT = T // 128
    NQT = T // 512
    ser = Ser()
    with ExitStack() as ctx:
        sb = lambda name, shape, dt=BF16: ctx.enter_context(nc.sbuf_tensor(name, shape, dt))
        ps = lambda name, shape, dt=F32: ctx.enter_context(nc.psum_tensor(name, shape, dt))
        q_sb = sb("q_sb", [128, 4, T])
        k_sb = sb("k_sb", [128, 4, T])
        v_sb = sb("v_sb2", [128, NKT, 512])
        vaug = sb("vaug2", [128, NKT, 8, 65])
        sq = sb("sq_sa", [128, T])
        pa = sb("pa", [128, 512])
        pb = sb("pb", [128, 512])
        araw = sb("araw", [64, 512], F32)
        asb = sb("asb", [64, 512])
        rec = sb("rec", [1, 512], F32)
        onesb = sb("onesb", [128, 1])
        onesf = sb("onesf", [1, 128], F32)
        hsc = sb("hsc", [1, 16], F32)
        ksc = sb("ksc", [1, 16], F32)
        bias_sb = sb("bias_sb", [128, 8], F32)
        psS_A = ps("psS_A", [128, 512])
        psS_B = ps("psS_B", [128, 512])
        psO_A = ps("psO_A", [128, 512])
        psO_B = ps("psO_B", [128, 512])
        psR = ps("psR", [128, 512])
        psQ = ps("psQ", [128, 512])

        ser.dma(lambda e: e.dma_start(out=q_sb[:], in_=q_d.rearrange("(c p) t -> p c t", p=128)))
        ser.dma(lambda e: e.dma_start(out=k_sb[:], in_=k_d.rearrange("(c p) t -> p c t", p=128)))
        ser.dma(lambda e: e.dma_start(out=v_sb[:], in_=v_d.rearrange("(kt p) d -> p kt d", p=128)))
        ser.vector(lambda e: e.memset(vaug[:], 1.0))
        for kt in range(NKT):
            for h in range(8):
                ser.vector(lambda e, kt=kt, h=h: e.tensor_copy(
                    vaug[:, kt, h, 0:64], v_sb[:, kt, 64 * h:64 * h + 64]))
        ser.vector(lambda e: e.memset(onesb[:], 1.0))
        ser.vector(lambda e: e.memset(onesf[:], 1.0))
        ser.vector(lambda e: e.memset(hsc[:], 0.0))
        ser.vector(lambda e: e.memset(ksc[:], 0.0))

        for src, acc in ((q_sb, hsc), (k_sb, ksc)):
            for c in range(4):
                ser.scalar(lambda e, c=c, src=src: e.activation(sq[:], src[:, c, :], AF.Square))
                for half in range(2):
                    h = 2 * c + half
                    pr = slice(64 * half, 64 * half + 64)
                    for n in range(NQT):
                        nsl = slice(512 * n, 512 * n + 512)
                        ser.tensor(lambda e, pr=pr, nsl=nsl: e.matmul(
                            psQ[0:1, :], onesb[pr, :], sq[pr, nsl], start=True, stop=True,
                            tile_position=(pr.start, 0)))
                        ser.vector(lambda e: e.reduce_max(rec[0:1, 0:1], psQ[0:1, :], AX.X))
                        ser.vector(lambda e, h=h, acc=acc: e.tensor_tensor(
                            acc[0:1, h:h + 1], acc[0:1, h:h + 1], rec[0:1, 0:1], ALU.max))
        ser.scalar(lambda e: e.activation(hsc[0:1, 8:16], hsc[0:1, 0:8], AF.Sqrt))
        ser.scalar(lambda e: e.activation(ksc[0:1, 8:16], ksc[0:1, 0:8], AF.Sqrt))
        for h in range(8):
            ser.vector(lambda e, h=h: e.tensor_tensor(
                hsc[0:1, h:h + 1], hsc[0:1, 8 + h:9 + h], ksc[0:1, 8 + h:9 + h], ALU.mult))
            ser.vector(lambda e, h=h: e.tensor_scalar_mul(hsc[0:1, h:h + 1], hsc[0:1, h:h + 1], -0.125))
            ser.tensor(lambda e, h=h: e.matmul(
                psR[:, 0:1], onesf[:, :], hsc[0:1, h:h + 1], start=True, stop=True))
            ser.vector(lambda e, h=h: e.tensor_copy(bias_sb[:, h:h + 1], psR[:, 0:1]))

        for p in range(4):
            for j in range(NQT):
                jsl = slice(512 * j, 512 * j + 512)
                for i in range(NKT):
                    isl = slice(128 * i, 128 * i + 128)
                    last = i == NKT - 1
                    ser.tensor(lambda e, p=p, isl=isl, jsl=jsl: e.matmul(
                        psS_A[:], k_sb[0:64, p, isl], q_sb[0:64, p, jsl],
                        start=True, stop=True, tile_position=(0, 0)))
                    ser.tensor(lambda e, p=p, isl=isl, jsl=jsl: e.matmul(
                        psS_B[:], k_sb[64:128, p, isl], q_sb[64:128, p, jsl],
                        start=True, stop=True, tile_position=(64, 0)))
                    ser.scalar(lambda e, p=p: e.activation(
                        pa[:], psS_A[:], AF.Exp, bias=bias_sb[:, 2 * p:2 * p + 1], scale=0.125))
                    ser.scalar(lambda e, p=p: e.activation(
                        pb[:], psS_B[:], AF.Exp, bias=bias_sb[:, 2 * p + 1:2 * p + 2], scale=0.125))
                    ser.tensor(lambda e, i=i, last=last, p=p: e.matmul(
                        psO_A[0:65, :], vaug[:, i, 2 * p, :], pa[:], start=(i == 0), stop=last))
                    ser.tensor(lambda e, i=i, last=last, p=p: e.matmul(
                        psO_B[0:65, :], vaug[:, i, 2 * p + 1, :], pb[:], start=(i == 0), stop=last))
                for half, psO in ((0, psO_A), (1, psO_B)):
                    h = 2 * p + half
                    ser.vector(lambda e, psO=psO: e.reciprocal(rec[:], psO[64:65, :]))
                    ser.tensor(lambda e: e.matmul(psR[0:64, :], onesf[:, 0:64], rec[:], start=True, stop=True))
                    ser.vector(lambda e, psO=psO: e.tensor_copy(araw[:], psO[0:64, :]))
                    ser.vector(lambda e: e.tensor_tensor(asb[:], araw[:], psR[0:64, :], ALU.mult))
                    ser.dma(lambda e, h=h, jsl=jsl: e.dma_start(
                        out=att_o[64 * h:64 * h + 64, jsl], in_=asb[:]))
        ser.emit(nc)
    return nc


def run_subatt(nc, q_f, k_f, v_tok):
    in_maps = []
    for c in range(NCORE):
        b, hg = c // 2, c % 2
        in_maps.append({
            "q_sl": np.ascontiguousarray(q_f[512 * hg:512 * hg + 512, T * b:T * b + T]),
            "k_sl": np.ascontiguousarray(k_f[512 * hg:512 * hg + 512, T * b:T * b + T]),
            "v_sl": np.ascontiguousarray(v_tok[T * b:T * b + T, 512 * hg:512 * hg + 512]),
        })
    res = _run(nc, in_maps)
    att = np.zeros((D, ROWS), BF)
    for c in range(NCORE):
        b, hg = c // 2, c % 2
        att[512 * hg:512 * hg + 512, T * b:T * b + T] = np.asarray(res[c]["att_sl"])
    return att



def build_subpost():
    nc = bass.Bass("TRN2", target_bir_lowering=False)
    z_d = nc.dram_tensor("z_rows", [RPC, D], F32, kind="ExternalInput").ap()
    att_d = nc.dram_tensor("att_sl", [D, RPC], BF16, kind="ExternalInput").ap()
    wo_d = nc.dram_tensor("wo", [D, D], F32, kind="ExternalInput").ap()
    g2_d = nc.dram_tensor("g2_row", [1, D], F32, kind="ExternalInput").ap()
    b2_d = nc.dram_tensor("b2_row", [1, D], F32, kind="ExternalInput").ap()
    w1_d = nc.dram_tensor("w1", [D, 4 * D], F32, kind="ExternalInput").ap()
    b1_d = nc.dram_tensor("b1", [1, 4 * D], F32, kind="ExternalInput").ap()
    w2_d = nc.dram_tensor("w2", [4 * D, D], F32, kind="ExternalInput").ap()
    id_d = nc.dram_tensor("ident", [128, 128], F32, kind="ExternalInput").ap()
    z2_o = nc.dram_tensor("z2", [RPC, D], F32, kind="ExternalOutput").ap()

    ser = Ser()
    with ExitStack() as ctx:
        sb = lambda name, shape, dt=BF16: ctx.enter_context(nc.sbuf_tensor(name, shape, dt))
        ps = lambda name, shape, dt=F32: ctx.enter_context(nc.psum_tensor(name, shape, dt))
        ident = sb("ident_s", [128, 128])
        wo_sb = sb("wo_sb", [128, 8, D])
        w1_sb = sb("w1_sb", [128, 8, 4 * D])
        w2_sb = sb("w2_sb", [128, 32, D])
        b1_r = sb("b1_r", [1, 4 * D])
        g2b = sb("g2b", [128, D], F32)
        b2b = sb("b2b", [128, D], F32)
        gbounce = sb("gbounce", [1, D], F32)
        onesf = sb("onesf", [1, 128], F32)
        ones = sb("ones_spo", [1, 512])
        attf_t = sb("attf_t", [128, 8, 128])
        z_sb = sb("z_sb", [128, D], F32)
        z1_sb = sb("z1_sb", [128, D], F32)
        zl2a = sb("zl2a", [128, D], F32)
        zl2b = sb("zl2b", [128, D])
        zlf = sb("zlf_t", [128, 8, 128])
        sg = sb("sg", [128, 128])
        act = sb("act", [128, 128])
        z2_sb = sb("z2_sb", [128, D], F32)
        sq = sb("sq", [128, D], F32)
        st = sb("st", [128, 8], F32)
        psO = ps("psO_spo", [128, 512])
        psA = ps("psA_spo", [128, 128])
        psZ0 = ps("psZ0", [128, 512])
        psZ1 = ps("psZ1", [128, 512])
        psT = ps("psT_spo", [128, 128], BF16)

        ser.dmag(lambda e: e.dma_start(out=ident[:], in_=id_d[:]))
        ser.dmag(lambda e: e.dma_start(out=wo_sb[:], in_=wo_d.rearrange("(c p) n -> p c n", p=128)))
        ser.dmag(lambda e: e.dma_start(out=w1_sb[:], in_=w1_d.rearrange("(c p) n -> p c n", p=128)))
        ser.dmag(lambda e: e.dma_start(out=w2_sb[:], in_=w2_d.rearrange("(c p) n -> p c n", p=128)))
        ser.dmag(lambda e: e.dma_start(out=b1_r[:], in_=b1_d[:]))
        ser.vector(lambda e: e.memset(onesf[:], 1.0))
        ser.vector(lambda e: e.memset(ones[:], 1.0))
        for row_d, dst in ((g2_d, g2b), (b2_d, b2b)):
            ser.dma(lambda e, row_d=row_d: e.dma_start(out=gbounce[:], in_=row_d[:]))
            for nb in range(2):
                nsl = slice(512 * nb, 512 * nb + 512)
                ser.tensor(lambda e, nsl=nsl: e.matmul(
                    psO[:], onesf[:, :], gbounce[:, nsl], start=True, stop=True))
                ser.vector(lambda e, dst=dst, nsl=nsl: e.tensor_copy(dst[:, nsl], psO[:]))

        tiles = {"sq": sq, "st": st}
        NT = RPC // 128
        for i in range(NT):
            r = slice(128 * i, 128 * i + 128)
            tsl = slice(128 * i, 128 * i + 128)
            ser.dma(lambda e, r=r: e.dma_start(out=z_sb[:], in_=z_d[r, :]))
            ser.dma(lambda e, tsl=tsl: e.dma_start(
                out=attf_t[:], in_=att_d.rearrange("(c p) t -> p c t", p=128)[:, :, tsl]))
            for nb in range(2):
                nsl = slice(512 * nb, 512 * nb + 512)
                for c in range(8):
                    ser.tensor(lambda e, c=c, nsl=nsl: e.matmul(
                        psO[:], attf_t[:, c, :], wo_sb[:, c, nsl], start=(c == 0), stop=(c == 7)))
                ser.vector(lambda e, nsl=nsl: e.tensor_tensor(
                    z1_sb[:, nsl], z_sb[:, nsl], psO[:], ALU.add))
            ln_tile(ser, None, z1_sb, zl2a, tiles)
            ser.vector(lambda e: e.tensor_tensor(zl2a[:], zl2a[:], g2b[:], ALU.mult))
            ser.vector(lambda e: e.tensor_tensor(zl2b[:], zl2a[:], b2b[:], ALU.add))
            for c in range(8):
                ser.tensor(lambda e, c=c: e.transpose(psT[:], zl2b[:, 128 * c:128 * c + 128], ident[:]))
                ser.vector(lambda e, c=c: e.tensor_copy(zlf[:, c, :], psT[:]))
            for mh in range(32):
                for c in range(8):
                    ser.tensor(lambda e, c=c, mh=mh: e.matmul(
                        psA[:], w1_sb[:, c, 128 * mh:128 * mh + 128], zlf[:, c, :],
                        start=(c == 0), stop=False))
                ser.tensor(lambda e, mh=mh: e.matmul(
                    psA[:], b1_r[:, 128 * mh:128 * mh + 128], ones[:, 0:128], start=False, stop=True))
                ser.scalar(lambda e: e.activation(sg[:], psA[:], AF.Sigmoid, scale=1.702))
                ser.vector(lambda e: e.tensor_tensor(act[:], sg[:], psA[:], ALU.mult))
                for nb, psZ in ((0, psZ0), (1, psZ1)):
                    nsl = slice(512 * nb, 512 * nb + 512)
                    ser.tensor(lambda e, mh=mh, nsl=nsl, psZ=psZ: e.matmul(
                        psZ[:], act[:], w2_sb[:, mh, nsl], start=(mh == 0), stop=(mh == 31)))
            for nb, psZ in ((0, psZ0), (1, psZ1)):
                nsl = slice(512 * nb, 512 * nb + 512)
                ser.vector(lambda e, nsl=nsl, psZ=psZ: e.tensor_tensor(
                    z2_sb[:, nsl], z1_sb[:, nsl], psZ[:], ALU.add))
            ser.dma(lambda e, r=r: e.dma_start(out=z2_o[r, :], in_=z2_sb[:]))
        ser.emit(nc)
    return nc


def run_subpost(nc, z, att_f, inp, li):
    in_maps = []
    for c in range(NCORE):
        r = slice(RPC * c, RPC * c + RPC)
        in_maps.append({
            "z_rows": np.ascontiguousarray(z[r]),
            "att_sl": np.ascontiguousarray(att_f[:, r]),
            "wo": np.asarray(inp["L1_attn_wo"], np.float32)[li],
            "g2_row": np.asarray(inp["L1_attn_ln2_g"], np.float32)[li].reshape(1, D),
            "b2_row": np.asarray(inp["L1_attn_ln2_b"], np.float32)[li].reshape(1, D),
            "w1": np.asarray(inp["L1_attn_w1"], np.float32)[li],
            "b1": np.asarray(inp["L1_attn_b1"], np.float32)[li].reshape(1, 4 * D),
            "w2": np.asarray(inp["L1_attn_w2"], np.float32)[li],
            "ident": IDENT,
        })
    res = _run(nc, in_maps)
    return np.concatenate([np.asarray(r_["z2"]) for r_ in res], axis=0)



QTH = (0.25, 0.75, 1.25, 1.75, 2.5, 3.5, 5.0)


def emit_quant(ser, src, dst, W, scr):
    scrW = scr["aw"].shape[1]
    if W > scrW:
        for a in range(0, W, scrW):
            w = min(scrW, W - a)
            emit_quant(ser, src[:, a:a + w], dst[:, a:a + w], w, scr)
        return
    nb = W // 16
    aw = scr["aw"][:, 0:W]
    ind = scr["ind"][:, 0:W]
    acc = scr["acc"][:, 0:W]
    sgn = scr["sgn"][:, 0:W]
    sblk = scr["sblk"][:, 0:nb]
    aw3 = aw.rearrange("p (b j) -> p b j", j=16)
    ind3 = ind.rearrange("p (b j) -> p b j", j=16)
    s3 = sblk[:, :, None].broadcast_to([128, nb, 16])
    ser.scalar(lambda e: e.activation(aw, src, AF.Abs))
    ser.scalar(lambda e: e.activation(sgn, src, AF.Sign))
    ser.vector(lambda e: e.tensor_reduce(sblk, aw3, AX.X, ALU.max))
    ser.vector(lambda e: e.tensor_scalar_mul(sblk, sblk, 1.0 / 6.0))
    ser.vector(lambda e: e.tensor_scalar_max(sblk, sblk, 1e-12))
    ser.vector(lambda e: e.scalar_tensor_tensor(
        out=acc.rearrange("p (b j) -> p b j", j=16), in0=s3, scalar=QTH[0], in1=aw3,
        op0=ALU.mult, op1=ALU.is_lt))
    for t in QTH[1:4]:
        ser.vector(lambda e, t=t: e.scalar_tensor_tensor(
            out=ind3, in0=s3, scalar=t, in1=aw3, op0=ALU.mult, op1=ALU.is_lt))
        ser.vector(lambda e: e.tensor_tensor(acc, acc, ind, ALU.add))
    for t, w in ((QTH[4], 2.0), (QTH[5], 2.0), (QTH[6], 4.0)):
        ser.vector(lambda e, t=t: e.scalar_tensor_tensor(
            out=ind3, in0=s3, scalar=t, in1=aw3, op0=ALU.mult, op1=ALU.is_lt))
        ser.vector(lambda e, w=w: e.scalar_tensor_tensor(
            out=acc, in0=ind, scalar=w, in1=acc, op0=ALU.mult, op1=ALU.add))
    ser.vector(lambda e: e.tensor_tensor(ind, acc, sgn, ALU.mult))
    ser.vector(lambda e: e.scalar_tensor_tensor(
        out=dst.rearrange("p (b j) -> p b j", j=16), in0=s3, scalar=0.5, in1=ind3,
        op0=ALU.mult, op1=ALU.mult))


def build_p9(nt=None):
    HI = 2816
    nc = bass.Bass("TRN2", target_bir_lowering=False)
    xn1_d = nc.dram_tensor("xn1_rows", [RPC, D], F32, kind="ExternalInput").ap()
    x1_d = nc.dram_tensor("x1_rows", [RPC, D], F32, kind="ExternalInput").ap()
    z4_d = nc.dram_tensor("z4_rows", [RPC, D], F32, kind="ExternalInput").ap()
    wg_d = nc.dram_tensor("w_gate", [HI, D], F32, kind="ExternalInput").ap()
    wu_d = nc.dram_tensor("w_up", [HI, D], F32, kind="ExternalInput").ap()
    wd_d = nc.dram_tensor("w_down", [D, HI], F32, kind="ExternalInput").ap()
    id_d = nc.dram_tensor("ident", [128, 128], F32, kind="ExternalInput").ap()
    out_o = nc.dram_tensor("out_rows", [RPC, D], F32, kind="ExternalOutput").ap()

    ser = Ser()
    with ExitStack() as ctx:
        sb = lambda name, shape, dt=BF16: ctx.enter_context(nc.sbuf_tensor(name, shape, dt))
        ps = lambda name, shape, dt=F32: ctx.enter_context(nc.psum_tensor(name, shape, dt))
        ident = sb("ident_s", [128, 128])
        wgqT = sb("wgqT", [128, 8, HI])
        wuqT = sb("wuqT", [128, 8, HI])
        wdqT = sb("wdqT", [128, 22, D])
        SW = HI // 2
        stage = sb("stage_c", [128, SW], F32)
        wqc = sb("wqc", [128, HI])
        scr = {"aw": sb("aw", [128, SW], F32), "ind": sb("ind_q", [128, SW]),
               "acc": sb("acc_q", [128, SW]), "sgn": sb("sgn_q", [128, SW]),
               "sblk": sb("sblk", [128, SW // 16], F32)}
        xq = sb("xq", [128, D])
        xqf = sb("xqf", [128, 8, 128])
        h_t = sb("h_t", [128, HI], F32)
        hq = sb("hq", [128, HI])
        hqf = sb("hqf", [128, 22, 128])
        sgh = sb("sgh", [128, 512])
        x1_sb = sb("x1_sb", [128, D], F32)
        z4_sb = sb("z4_sb", [128, D], F32)
        xn1_sb = sb("xn1_sb", [128, D], F32)
        out_sb = sb("out_sb", [128, D], F32)
        psG = ps("psG", [128, 512])
        psU = ps("psU", [128, 512])
        psD = ps("psD", [128, 512])
        psT = ps("psT_p9", [128, 128], BF16)

        ser.dmag(lambda e: e.dma_start(out=ident[:], in_=id_d[:]))
        for (w_d, wT, ncw, W) in ((wg_d, wgqT, 22, D), (wu_d, wuqT, 22, D), (wd_d, wdqT, 8, HI)):
            for co in range(ncw):
                for a in range(0, W, SW):
                    w_ = min(SW, W - a)
                    ser.dma(lambda e, w_d=w_d, co=co, a=a, w_=w_: e.dma_start(
                        out=stage[:, 0:w_], in_=w_d[128 * co:128 * co + 128, a:a + w_]))
                    emit_quant(ser, stage[:, 0:w_], wqc[:, a:a + w_], w_, scr)
                for ci in range(W // 128):
                    ser.tensor(lambda e, ci=ci: e.transpose(
                        psT[:], wqc[:, 128 * ci:128 * ci + 128], ident[:]))
                    ser.vector(lambda e, ci=ci, co=co, wT=wT: e.tensor_copy(
                        wT[:, ci, 128 * co:128 * co + 128], psT[:]))

        gate_blocks = [(0, 512), (512, 512), (1024, 512), (1536, 512), (2048, 512), (2560, 256)]
        NT = nt if nt is not None else RPC // 128
        for i in range(NT):
            r = slice(128 * i, 128 * i + 128)
            ser.dma(lambda e, r=r: e.dma_start(out=xn1_sb[:], in_=xn1_d[r, :]))
            ser.dma(lambda e, r=r: e.dma_start(out=x1_sb[:], in_=x1_d[r, :]))
            ser.dma(lambda e, r=r: e.dma_start(out=z4_sb[:], in_=z4_d[r, :]))
            emit_quant(ser, xn1_sb[:, :], xq[:, :], D, scr)
            for c in range(8):
                ser.tensor(lambda e, c=c: e.transpose(psT[:], xq[:, 128 * c:128 * c + 128], ident[:]))
                ser.vector(lambda e, c=c: e.tensor_copy(xqf[:, c, :], psT[:]))
            for o0, ow in gate_blocks:
                osl = slice(o0, o0 + ow)
                for c in range(8):
                    ser.tensor(lambda e, c=c, osl=osl, ow=ow: e.matmul(
                        psG[:, 0:ow], xqf[:, c, :], wgqT[:, c, osl], start=(c == 0), stop=(c == 7)))
                for c in range(8):
                    ser.tensor(lambda e, c=c, osl=osl, ow=ow: e.matmul(
                        psU[:, 0:ow], xqf[:, c, :], wuqT[:, c, osl], start=(c == 0), stop=(c == 7)))
                ser.scalar(lambda e, ow=ow: e.activation(sgh[:, 0:ow], psG[:, 0:ow], AF.Silu))
                ser.vector(lambda e, osl=osl, ow=ow: e.tensor_tensor(
                    h_t[:, osl], sgh[:, 0:ow], psU[:, 0:ow], ALU.mult))
            emit_quant(ser, h_t[:, :], hq[:, :], HI, scr)
            for c in range(22):
                ser.tensor(lambda e, c=c: e.transpose(psT[:], hq[:, 128 * c:128 * c + 128], ident[:]))
                ser.vector(lambda e, c=c: e.tensor_copy(hqf[:, c, :], psT[:]))
            for nb in range(2):
                nsl = slice(512 * nb, 512 * nb + 512)
                for c in range(22):
                    ser.tensor(lambda e, c=c, nsl=nsl: e.matmul(
                        psD[:], hqf[:, c, :], wdqT[:, c, nsl], start=(c == 0), stop=(c == 21)))
                ser.vector(lambda e, nsl=nsl: e.tensor_tensor(
                    out_sb[:, nsl], x1_sb[:, nsl], z4_sb[:, nsl], ALU.add))
                ser.vector(lambda e, nsl=nsl: e.tensor_tensor(
                    out_sb[:, nsl], out_sb[:, nsl], psD[:], ALU.add))
            ser.dma(lambda e, r=r: e.dma_start(out=out_o[r, :], in_=out_sb[:]))
        ser.emit(nc)
    return nc


def run_p9(xn1, x1, z4, inp):
    nc = build_p9()
    in_maps = []
    for c in range(NCORE):
        r = slice(RPC * c, RPC * c + RPC)
        in_maps.append({
            "xn1_rows": np.ascontiguousarray(xn1[r]),
            "x1_rows": np.ascontiguousarray(x1[r]),
            "z4_rows": np.ascontiguousarray(z4[r]),
            "w_gate": np.asarray(inp["L1_ffn_w_gate"], np.float32),
            "w_up": np.asarray(inp["L1_ffn_w_up"], np.float32),
            "w_down": np.asarray(inp["L1_ffn_w_down"], np.float32),
            "ident": IDENT,
        })
    res = _run(nc, in_maps)
    return np.concatenate([np.asarray(r_["out_rows"]) for r_ in res], axis=0)
